# version 29
# baseline (speedup 1.0000x reference)
import contextlib
import numpy as np

import concourse.bass as bass
import concourse.mybir as mybir
from concourse.bass_utils import run_bass_kernel_spmd

F32 = mybir.dt.float32
BF16 = mybir.dt.bfloat16
U32 = mybir.dt.uint32
AF = mybir.ActivationFunctionType
OP = mybir.AluOpType
AX = mybir.AxisListType

T = 2048
D = 1024
H = 16
C = 64
FF = 4096
E = 8
R = 4
NT = T // 128
DC = D // 128
NCORES = 8
EPS = 1e-5
MSHIFT = -20.0
NEG = -1.0e30
EARLY_EPI = True


class KB:
    def __init__(self, debug=False):
        self.debug = debug
        self.nc = bass.Bass("TRN2", target_bir_lowering=False)
        self.cnt = {}
        self.seen = {e: {} for e in ("T", "V", "S", "G", "Y")}
        self.sems = {}

    def inc(self, instr, sem, n=1):
        self.cnt[sem] += n
        instr.then_inc(self.sems[sem], n)
        return (sem, self.cnt[sem])

    def w(self, eng, *toks):
        for tok in toks:
            if tok is None:
                continue
            sem, val = tok
            if val <= 0:
                continue
            assert val <= self.cnt[sem], f"wait on future {sem}={val}>{self.cnt[sem]}"
            s = self.seen[eng]
            if s.get(sem, 0) >= val:
                continue
            self.eng[eng].wait_ge(self.sems[sem], val)
            s[sem] = val

    def gload(self, dst, src):
        self.w("G", ("dG", self.cnt["dG"]))
        return self.inc(self.nc.gpsimd.dma_start(out=dst, in_=src), "dG", 16)

    def sload(self, dst, src):
        self.w("Y", ("dS", self.cnt["dS"]))
        return self.inc(self.nc.sync.dma_start(out=dst, in_=src), "dS", 16)

    def stransp(self, dst, src):
        self.w("Y", ("dS", self.cnt["dS"]))
        return self.inc(self.nc.sync.dma_start(out=dst, in_=src, transpose=True),
                        "dS", 16)

    def xload(self, dst, src, g):
        return self.inc(self.nc.sync.dma_start(out=dst, in_=src),
                        f"dX{g}", 16)

    def tload(self, dst, src, g):
        return self.inc(self.nc.sync.dma_start(out=dst, in_=src, transpose=True),
                        f"dU{g}", 16)

    def build(self):
        nc = self.nc
        es = contextlib.ExitStack()
        A = es.enter_context
        self.eng = {"T": nc.tensor, "V": nc.vector, "S": nc.scalar,
                    "G": nc.gpsimd, "Y": nc.sync}
        for name in (["dG", "dS", "tT", "tV", "tS", "tG"]
                     + [f"dX{i}" for i in range(4)]
                     + [f"dU{i}" for i in range(8)]):
            self.sems[name] = A(nc.semaphore(name))
            self.cnt[name] = 0

        self.d_x = nc.dram_tensor("x", [T, D], F32, kind="ExternalInput").ap()
        self.d_wqT = nc.dram_tensor("wqT", [C, C], F32, kind="ExternalInput").ap()
        self.d_wkT = nc.dram_tensor("wkT", [C, C], F32, kind="ExternalInput").ap()
        self.d_wvT = nc.dram_tensor("wvT", [C, C], F32, kind="ExternalInput").ap()
        self.d_wuT = nc.dram_tensor("wuT", [128, DC * D], F32, kind="ExternalInput").ap()
        self.d_bu = nc.dram_tensor("bu", [1, D], F32, kind="ExternalInput").ap()
        self.d_gwT = nc.dram_tensor("gwT", [128, DC * E], F32, kind="ExternalInput").ap()
        self.d_gb = nc.dram_tensor("gb", [E, 1], F32, kind="ExternalInput").ap()
        self.d_wiT = nc.dram_tensor("wiT", [128, 4 * DC * 1024], F32, kind="ExternalInput").ap()
        self.d_wib = nc.dram_tensor("wib", [128, FF // 128], F32, kind="ExternalInput").ap()
        self.d_woT = nc.dram_tensor("woT", [128, 4 * 8 * 1024], F32, kind="ExternalInput").ap()
        self.d_wob = nc.dram_tensor("wob", [1, D], F32, kind="ExternalInput").ap()
        self.d_acatT = nc.dram_tensor("acatT", [128, DC * E * R], F32, kind="ExternalInput").ap()
        self.d_bcatT = nc.dram_tensor("bcatT", [E * R, FF], F32, kind="ExternalInput").ap()
        self.d_maskT = nc.dram_tensor("cmaskT", [128, 128], F32, kind="ExternalInput").ap()
        self.d_iota = nc.dram_tensor("ciota", [128, E * R], F32, kind="ExternalInput").ap()
        self.d_qksc = nc.dram_tensor("cqksc", [128, 2], F32, kind="ExternalInput").ap()
        self.d_ident = nc.dram_tensor("cident", [128, 128], F32, kind="ExternalInput").ap()
        self.d_ones32 = nc.dram_tensor("cones32", [128, 128], F32, kind="ExternalInput").ap()
        self.d_id16 = nc.dram_tensor("cid16", [128, 128], BF16, kind="ExternalInput").ap()
        self.d_tri16 = nc.dram_tensor("ctri16", [128, 128], BF16, kind="ExternalInput").ap()
        self.d_ones16 = nc.dram_tensor("cones16", [128, NT * (C + 1)], BF16, kind="ExternalInput").ap()
        d_out = nc.dram_tensor("out", [T, D], F32, kind="ExternalOutput").ap()
        if self.debug:
            self.d_dbg_xh = nc.dram_tensor("dbg_xh", [128, DC * T], BF16, kind="ExternalOutput").ap()
            self.d_dbg_lg = nc.dram_tensor("dbg_lg", [E, T], F32, kind="ExternalOutput").ap()
            self.d_dbg_zm = nc.dram_tensor("dbg_zm", [E * R, T], BF16, kind="ExternalOutput").ap()
            self.d_dbg_y = nc.dram_tensor("dbg_y", [128, DC * T], BF16, kind="ExternalOutput").ap()
            self.d_dbg_acc = nc.dram_tensor("dbg_acc", [128, NT * D], F32, kind="ExternalOutput").ap()
            self.d_dbg_q = nc.dram_tensor("dbg_q", [64, T], BF16, kind="ExternalOutput").ap()
            self.d_dbg_k = nc.dram_tensor("dbg_k", [64, T], BF16, kind="ExternalOutput").ap()
            self.d_dbg_rs = nc.dram_tensor("dbg_rs", [65, 1024], F32, kind="ExternalOutput").ap()
        self.d_x_t = self.d_x.rearrange("(n p) m -> n p m", p=128)
        self.d_out_t = d_out.rearrange("(n p) m -> n p m", p=128)

        self.acc = A(nc.sbuf_tensor("acc", [128, NT, D], F32))
        self.bias_bc = A(nc.sbuf_tensor("bias_bc", [128, D], F32))
        self.xh_fm = A(nc.sbuf_tensor("xh_fm", [128, DC, T], BF16))
        self.wiq = [A(nc.sbuf_tensor("wiq0", [128, DC, 1024], BF16)), None]
        self.woq = [A(nc.sbuf_tensor("woq0", [128, 8, 1024], BF16)), None]
        self.zm_sb = A(nc.sbuf_tensor("zm_sb", [E * R, T], BF16))
        self.wq_sb = A(nc.sbuf_tensor("wq_sb", [128, C], BF16))
        self.wk_sb = A(nc.sbuf_tensor("wk_sb", [128, C], BF16))
        self.wv_sb = A(nc.sbuf_tensor("wv_sb", [128, C], BF16))
        self.gwh = A(nc.sbuf_tensor("gwh", [128, DC, E], BF16))
        self.gwl = A(nc.sbuf_tensor("gwl", [128, DC, E], BF16))
        self.gw32 = A(nc.sbuf_tensor("gw32", [128, DC, E], F32))
        self.gb_sb = A(nc.sbuf_tensor("gb_sb", [E, 1], F32))
        self.acat_sb = A(nc.sbuf_tensor("acat_sb", [128, DC, E * R], BF16))
        self.wib_sb = A(nc.sbuf_tensor("wib_sb", [128, FF // 128], F32))
        self.maskT = A(nc.sbuf_tensor("maskT_sb", [128, 128], F32))
        self.qksc = A(nc.sbuf_tensor("qksc_sb", [128, 2], F32))
        self.ident = A(nc.sbuf_tensor("ident_sb", [128, 128], F32))
        self.onesf = A(nc.sbuf_tensor("onesf", [128, 128], F32))
        self.id16 = A(nc.sbuf_tensor("id16_sb", [128, 128], BF16))
        self.tri16 = A(nc.sbuf_tensor("tri16_sb", [128, 128], BF16))

        self.v_lo = A(nc.sbuf_tensor("v_lo", [128, NT, C + 1], BF16))
        self.v_tmp = A(nc.sbuf_tensor("v_tmp", [128, NT, C], BF16))
        self.onesb = A(nc.sbuf_tensor("onesb", [128, 64], BF16))

        self.pA = A(nc.psum_tensor("pA", [128, 2048], F32))
        self.pB = A(nc.psum_tensor("pB", [128, 2048], F32))

        with contextlib.ExitStack() as es1:
            A1 = es1.enter_context
            self.sq_sc = A1(nc.sbuf_tensor("sq_sc", [128, D], F32))
            self.sv = A1(nc.sbuf_tensor("sv", [128, 6, NT], F32))
            self.st1 = A1(nc.sbuf_tensor("st1", [128, NT], F32))
            self.st2 = A1(nc.sbuf_tensor("st2", [128, NT], F32))
            self.max8 = A1(nc.sbuf_tensor("max8", [128, 8], F32))
            self.idx8 = A1(nc.sbuf_tensor("idx8", [128, 8], U32))
            self.idxf = A1(nc.sbuf_tensor("idxf", [128, 1], F32))
            self.lgt_tm = A1(nc.sbuf_tensor("lgt_tm", [128, NT, E], F32))
            self.oh32 = A1(nc.sbuf_tensor("oh32", [128, NT, E * R], F32))
            self.iota32 = A1(nc.sbuf_tensor("iota32_sb", [128, E * R], F32))
            self.xn32 = [A1(nc.sbuf_tensor(f"xn32_{i}", [128, D], F32)) for i in range(2)]
            self.xn16 = [A1(nc.sbuf_tensor(f"xn16_{i}", [128, D], BF16)) for i in range(2)]
            self.xl16 = [A1(nc.sbuf_tensor(f"xl16_{i}", [128, D], BF16)) for i in range(2)]
            self.xlfm = [A1(nc.sbuf_tensor(f"xlfm_{i}", [128, DC, 128], BF16)) for i in range(2)]
            self.logits_sb = A1(nc.sbuf_tensor("logits_sb", [E, T], F32))
            self.mz_sb = A1(nc.sbuf_tensor("mz_sb", [E * R, T], BF16))
            with nc.Block() as blk1:
                @blk1.sync
                def _(_sync):
                    self.emit_p0()
                    self.emit_p1()
                    self.emit_p15()
        with contextlib.ExitStack() as esy:
            self.y_fm = esy.enter_context(nc.sbuf_tensor("y_fm", [128, DC, T], BF16))
            with contextlib.ExitStack() as es2:
                A2 = es2.enter_context
                self.q_sb = A2(nc.sbuf_tensor("q_sb", [128, T], BF16))
                self.k_sb = A2(nc.sbuf_tensor("k_sb", [128, T], BF16))
                self.pt_sb = A2(nc.sbuf_tensor("pt_sb", [128, 8 * 512], BF16))
                self.rsbc = A2(nc.sbuf_tensor("rsbc", [128, 1024], F32))
                self.ytmp = A2(nc.sbuf_tensor("ytmp", [128, 2048], BF16))
                self.r16b = A2(nc.sbuf_tensor("r16b", [128, 1024], BF16))
                with nc.Block() as blk2:
                    @blk2.sync
                    def _(_sync):
                        self.emit_att()
            with contextlib.ExitStack() as es3a:
                self.wuT_sb = es3a.enter_context(nc.sbuf_tensor("wuT_sb", [128, DC, D], BF16))
                with nc.Block() as blk3:
                    @blk3.sync
                    def _(_sync):
                        self.emit_wu()
        with contextlib.ExitStack() as es3:
            A3 = es3.enter_context
            self.wiq[1] = A3(nc.sbuf_tensor("wiq1", [128, DC, 1024], BF16))
            self.woq[1] = A3(nc.sbuf_tensor("woq1", [128, 8, 1024], BF16))
            self.bcat_sb = A3(nc.sbuf_tensor("bcat_sb", [E * R, FF], BF16))
            self.inter = [A3(nc.sbuf_tensor(f"inter{i}", [128, 8, 512], BF16)) for i in range(2)]
            with nc.Block() as blk4:
                @blk4.sync
                def _(_sync):
                    self.emit_ffn()

        es.close()
        return nc

    def emit_p0(self):
        nc, w, inc = self.nc, self.w, self.inc
        TN, VN, SN, GN = nc.tensor, nc.vector, nc.scalar, nc.gpsimd
        self.ld_x = []
        for wv in range(4):
            for i in range(4):
                tok = self.xload(self.acc[:, wv * 4 + i, :],
                                 self.d_x_t[wv * 4 + i], wv)
            self.ld_x += [tok] * 4

        self.burow = self.logits_sb[0:1, 0:D]
        self.wobrow = self.logits_sb[0:1, D:2 * D]
        self.brow = self.sq_sc[0:1, 0:D]

        self.gload(self.onesf[:], self.d_ones32)
        self.gload(self.onesb[:], self.d_ones16[:, 0:64])
        self.gload(self.id16[:], self.d_id16)
        self.ld_tri = self.gload(self.tri16[:], self.d_tri16)
        self.e_memset = self.gload(self.v_lo.ap().rearrange("p k c -> p (k c)"),
                                   self.d_ones16)
        ld_gw32 = self.gload(self.gw32[:], self.d_gwT.rearrange("p (c e) -> p c e", c=DC))
        ld_gwh = self.gload(self.gwh[:], self.d_gwT.rearrange("p (c e) -> p c e", c=DC))
        ld_bu = self.gload(self.burow, self.d_bu)
        ld_wob = self.gload(self.wobrow, self.d_wob)
        self.ld_iota = self.gload(self.iota32[:], self.d_iota)
        self.ld_qksc = self.gload(self.qksc[:], self.d_qksc)
        self.ld_ident = self.gload(self.ident[:], self.d_ident)
        self.gload(self.wq_sb[0:64, :], self.d_wqT)
        self.gload(self.wk_sb[0:64, :], self.d_wkT)
        self.gload(self.wv_sb[0:64, :], self.d_wvT)
        self.ld_wq = self.gload(self.wq_sb[64:128, :], self.d_wqT)
        self.ld_wk = self.gload(self.wk_sb[64:128, :], self.d_wkT)
        self.ld_wv = self.gload(self.wv_sb[64:128, :], self.d_wvT)
        self.ld_gb = self.gload(self.gb_sb[:], self.d_gb)
        self.ld_wib = self.gload(self.wib_sb[:], self.d_wib)
        self.ld_acat = self.gload(self.acat_sb[:],
                                  self.d_acatT.rearrange("p (c j) -> p c j", c=DC))
        self.ld_w = [None] * 4
        self.gload(self.wiq[0][:], self.d_wiT[:, 0:8192].rearrange("p (c f) -> p c f", c=DC))
        self.ld_w[0] = self.gload(self.woq[0][:],
                                  self.d_woT[:, 0:8192].rearrange("p (c f) -> p c f", c=8))
        self._p0v = (ld_gw32, ld_gwh, ld_bu, ld_wob)

    def emit_p1(self):
        nc, w, inc = self.nc, self.w, self.inc
        TN, VN, SN = nc.tensor, nc.vector, nc.scalar
        sv, st1, st2 = self.sv, self.st1, self.st2
        self.e_gate = [None] * NT
        self.e_trn = [None] * NT
        self.e_trl = [None] * NT
        e_xlv = [None] * NT
        e_sq = [None] * NT
        for tt in range(NT):
            w("V", self.ld_x[tt])
            inc(VN.reduce_sum(st1[:, tt:tt + 1], self.acc[:, tt, :], axis=AX.X), "tV")
            if tt == 8:
                ld_gw32, ld_gwh, ld_bu, ld_wob = self._p0v
                w("V", ld_gw32, ld_gwh)
                self.e_gwl = inc(VN.tensor_tensor(self.gwl[:], self.gw32[:],
                                                  self.gwh[:], OP.subtract), "tV")
                w("V", ld_bu, ld_wob)
                e_brow = inc(VN.tensor_tensor(self.brow, self.burow, self.wobrow,
                                              OP.add), "tV")
                w("T", e_brow)
                TN.matmul(self.pA[:, 0:512], self.onesf[0:1, :], self.brow[:, 0:512],
                          start=True, stop=True)
                self.e_bmm = inc(TN.matmul(self.pA[:, 512:1024], self.onesf[0:1, :],
                                           self.brow[:, 512:1024],
                                           start=True, stop=True), "tT")
                w("V", self.e_bmm)
                self.e_bias = inc(VN.tensor_copy(self.bias_bc[:], self.pA[:, 0:1024]),
                                  "tV")
            w("S", self.ld_x[tt])
            if tt >= 1:
                w("S", e_sq[tt - 1])
            e_sq[tt] = inc(SN.activation(self.xn32[0][:], self.acc[:, tt, :], AF.Square,
                                         accum_out=st2[:, tt:tt + 1]), "tS")
        e_sum = ("tV", self.cnt["tV"])
        w("V", e_sum, e_sq[NT - 1])
        e1 = inc(VN.tensor_scalar(sv[:, 0, :], st1[:], 1.0 / D, None, OP.mult), "tV")
        w("V", e1)
        e2 = inc(VN.tensor_tensor(sv[:, 1, :], sv[:, 0, :], sv[:, 0, :], OP.mult), "tV")
        w("V", e2)
        e3 = inc(VN.scalar_tensor_tensor(sv[:, 1, :], st2[:], 1.0 / D,
                                         sv[:, 1, :], OP.mult, OP.subtract), "tV")
        w("V", e3)
        e4 = inc(VN.tensor_scalar(sv[:, 1, :], sv[:, 1, :], EPS, None, OP.add), "tV")
        w("S", e4)
        e5 = inc(SN.activation(sv[:, 2, :], sv[:, 1, :], AF.Sqrt), "tS")
        w("V", e5)
        e6 = inc(VN.reciprocal(sv[:, 3, :], sv[:, 2, :]), "tV")
        w("V", e6)
        e7 = inc(VN.tensor_tensor(sv[:, 4, :], sv[:, 3, :], sv[:, 3, :], OP.mult), "tV")
        w("V", e7)
        e8 = inc(VN.tensor_tensor(sv[:, 4, :], sv[:, 4, :], sv[:, 1, :], OP.mult), "tV")
        w("V", e8)
        e9 = inc(VN.tensor_scalar(sv[:, 4, :], sv[:, 4, :], -0.5, 1.5, OP.mult, OP.add), "tV")
        w("V", e9)
        e10 = inc(VN.tensor_tensor(sv[:, 2, :], sv[:, 3, :], sv[:, 4, :], OP.mult), "tV")
        w("V", e10)
        e11 = inc(VN.scalar_tensor_tensor(sv[:, 5, :], sv[:, 0, :], -1.0,
                                          sv[:, 2, :], OP.mult, OP.mult), "tV")
        for tt in range(NT):
            bu2 = tt % 2
            wv = tt // 2
            w("S", e11)
            if tt >= 2:
                w("S", e_xlv[tt - 2])
            e_xn32 = inc(SN.activation(self.xn32[bu2][:], self.acc[:, tt, :], AF.Identity,
                                       bias=sv[:, 5, tt:tt + 1], scale=sv[:, 2, tt:tt + 1]), "tS")
            w("S", e_xn32)
            if tt >= 2:
                w("S", self.e_trn[tt - 2])
            e_x16 = inc(SN.activation(self.xn16[bu2][:], self.xn32[bu2][:],
                                      AF.Identity), "tS")
            w("V", e_x16)
            if tt >= 2:
                w("V", self.e_trl[tt - 2])
            e_xlv[tt] = inc(VN.tensor_tensor(self.xl16[bu2][:], self.xn32[bu2][:],
                                             self.xn16[bu2][:], OP.subtract), "tV")
            w("Y", e_xlv[tt], e_x16)
            if tt >= 2:
                w("Y", self.e_gate[tt - 2])
            self.tload(self.xh_fm[:, :, tt * 128:(tt + 1) * 128],
                       self.xn16[bu2].ap().rearrange("p (c f) -> p c f", c=DC), wv)
            tok = self.tload(
                self.xlfm[bu2][:], self.xl16[bu2].ap().rearrange("p (c f) -> p c f", c=DC), wv)
            if bu2 == 0:
                continue
            self.e_trn[tt - 1] = self.e_trn[tt] = tok
            self.e_trl[tt - 1] = self.e_trl[tt] = tok
            for t2 in (tt - 1, tt):
                b2 = t2 % 2
                w("T", tok, self.e_gwl)
                first = True
                for (xs, gs) in ((None, self.gwh), (None, self.gwl), (self.xlfm[b2], self.gwh)):
                    for dc in range(DC):
                        rhs = (self.xh_fm[:, dc, t2 * 128:(t2 + 1) * 128] if xs is None
                               else xs[:, dc, :])
                        mm = TN.matmul(self.pB[0:E, t2 * 128:(t2 + 1) * 128], gs[:, dc, :], rhs,
                                       start=first, stop=(dc == DC - 1 and xs is not None),
                                       skip_group_check=True)
                        first = False
                self.e_gate[t2] = inc(mm, "tT")

    def emit_p15(self):
        nc, w, inc = self.nc, self.w, self.inc
        TN, VN, SN = nc.tensor, nc.vector, nc.scalar
        w("S", self.e_gate[NT - 1], self.ld_gb)
        e_lg = inc(SN.activation(self.logits_sb[:], self.pB[0:E, :], AF.Identity,
                                 bias=self.gb_sb[:]), "tS")
        w("T", e_lg, self.ld_ident, self.e_bias)
        for tt in range(NT):
            mm = TN.transpose(self.pA[:, tt * E:(tt + 1) * E],
                              self.logits_sb[0:E, tt * 128:(tt + 1) * 128],
                              self.ident[0:E, 0:E])
        e_lgt = inc(mm, "tT")
        w("V", e_lgt)
        e_lgtv = inc(VN.tensor_copy(
            self.lgt_tm[:], self.pA[:, 0:NT * E].rearrange("p (n e) -> p n e", n=NT)), "tV")
        e_oh = [None] * NT
        for tt in range(NT):
            w("V", e_lgtv, self.ld_iota)
            ea = inc(VN.max(self.max8[:], self.lgt_tm[:, tt, :]), "tV")
            w("V", ea)
            eb = inc(VN.max_index(self.idx8[:], self.max8[:], self.lgt_tm[:, tt, :]), "tV")
            w("V", eb)
            ec = inc(VN.tensor_copy(self.idxf[:], self.idx8[:, 0:1]), "tV")
            w("V", ec)
            e_oh[tt] = inc(VN.tensor_scalar(self.oh32[:, tt, :], self.iota32[:],
                                            self.idxf[:], None, OP.is_equal), "tV")
        w("T", e_lg)
        for tt in range(NT):
            w("T", e_oh[tt])
            mm = TN.transpose(self.pB[0:E * R, tt * 128:(tt + 1) * 128],
                              self.oh32[:, tt, :], self.ident[:])
        e_mzt = inc(mm, "tT")
        w("V", e_mzt)
        e_mz = inc(VN.tensor_copy(self.mz_sb[:], self.pB[0:E * R, :]), "tV")
        w("T", self.ld_acat, e_lgtv, self.e_trn[NT - 1], self.e_trl[NT - 1])
        for dc in range(DC):
            for ck in range(4):
                mm = TN.matmul(self.pA[0:E * R, ck * 512:(ck + 1) * 512],
                               self.acat_sb[:, dc, :],
                               self.xh_fm[:, dc, ck * 512:(ck + 1) * 512],
                               start=(dc == 0), stop=(dc == DC - 1), skip_group_check=True)
        e_z = inc(mm, "tT")
        w("V", e_z, e_mz)
        self.e_zm = inc(VN.tensor_tensor(self.zm_sb[:], self.pA[0:E * R, :],
                                         self.mz_sb[:], OP.mult), "tV")
        if self.debug:
            w("Y", self.e_zm, e_lg, self.e_trn[NT - 1], self.e_trl[NT - 1])
            self.sload(self.d_dbg_xh, self.xh_fm.ap().rearrange("p c t -> p (c t)"))
            self.sload(self.d_dbg_lg, self.logits_sb[:])
            self.sload(self.d_dbg_zm, self.zm_sb[:])

    def emit_att(self):
        nc, w, inc = self.nc, self.w, self.inc
        TN, VN, SN, GN = nc.tensor, nc.vector, nc.scalar, nc.gpsimd
        self.e_ynorm = [None] * H
        self.e_rsev = [None] * H
        self.e_exp_last = [None] * H
        e_pv_last = [None] * H
        e_pb_last = [None] * H
        e_yev_last = [None] * H
        e_qev = [None] * H
        e_slast = [None] * H

        def hsl(hh):
            return self.xh_fm[(hh & 1) * 64:((hh & 1) + 1) * 64, hh >> 1, :]

        def emit_qproj(hh, e4):
            par2 = hh & 1
            wqh = self.wq_sb[par2 * 64:(par2 + 1) * 64, :]
            w("T", self.ld_wq, self.e_trn[NT - 1])
            for ck in range(4):
                if e4 is not None:
                    w("T", e4[ck])
                mmq = TN.matmul(self.pA[0:64, ck * 512:(ck + 1) * 512], wqh,
                                hsl(hh)[:, ck * 512:(ck + 1) * 512], start=True, stop=True)
            return inc(mmq, "tT")

        def emit_qdup(hh, e_q):
            w("V", e_q)
            ev = inc(VN.tensor_copy(self.q_sb[0:64, :], self.pA[0:64, :]), "tV")
            w("Y", ev)
            if hh >= 1:
                w("Y", e_slast[hh - 1])
            w("Y", ("dS", self.cnt["dS"]))
            ed = inc(nc.sync.dma_start(out=self.q_sb[64:128, :],
                                       in_=self.q_sb[0:64, :]), "dS", 16)
            return ev, ed

        def emit_vproj(hh, e_ln1, e_yev1, e_kev2):
            w("T", self.ld_wv, self.e_trn[NT - 1])
            wvh = self.wv_sb[(hh & 1) * 64:((hh & 1) + 1) * 64, :]
            if hh >= 1:
                w("T", e_yev1)
                w("T", e_ln1)
            w("T", e_kev2)
            for kt in range(NT):
                mm = TN.matmul(self.pB[:, 1024 + kt * 64:1024 + (kt + 1) * 64],
                               hsl(hh)[:, kt * 128:(kt + 1) * 128], wvh,
                               start=True, stop=True)
            return inc(mm, "tT")

        def emit_vev(hh, e_vmm):
            w("V", e_vmm)
            if hh >= 2:
                w("V", e_pv_last[hh - 2])
            vdst = self.v_lo[:, :, 0:C] if (hh & 1) == 0 else self.v_tmp[:]
            return inc(VN.tensor_copy(
                vdst, self.pB[:, 1024:2048].rearrange("p (k c) -> p k c", k=NT)), "tV")

        def emit_kchain(hh, kr, e_vev_tok, e_ln1):
            wkh = self.wk_sb[(hh & 1) * 64:((hh & 1) + 1) * 64, :]
            hs2 = hsl(hh)
            w("T", self.ld_wk, e_vev_tok, e_ln1)
            for ck in range(4):
                mm = TN.matmul(self.pB[kr:kr + 64, ck * 512:(ck + 1) * 512], wkh,
                               hs2[:, ck * 512:(ck + 1) * 512], start=True, stop=True)
            e_qk = inc(mm, "tT")
            w("V", e_qk)
            e_kev_hi = inc(VN.tensor_scalar(self.k_sb[kr:kr + 64, 1024:2048],
                                            self.pB[kr:kr + 64, 1024:2048],
                                            1.0 / 32.0, None, OP.mult), "tV")
            e_kev = inc(VN.tensor_scalar(self.k_sb[kr:kr + 64, 0:1024],
                                         self.pB[kr:kr + 64, 0:1024], 1.0 / 32.0,
                                         None, OP.mult), "tV")
            w("Y", e_kev)
            if hh >= 1:
                w("Y", e_slast[hh - 1])
            w("Y", ("dS", self.cnt["dS"]))
            e_kdup = inc(nc.sync.dma_start(out=self.k_sb[64 - kr:128 - kr, :],
                                           in_=self.k_sb[kr:kr + 64, :]), "dS", 16)
            return e_kev, e_kdup, e_kev_hi

        e_vev_h = [None] * H
        e_kev_h = [None] * H
        e_kdup_h = [None] * H
        e_qn = emit_qproj(0, None)
        e_qev[0] = emit_qdup(0, e_qn)
        for h in range(H):
            par = h & 1
            ch = h >> 1
            lo = (par == 0)
            hs = hsl(h)
            kr = par * 64
            if h == 0:
                e_kev_h[0], e_kdup_h[0], ekh = emit_kchain(0, 0, None, None)
                e_vev_h[0] = emit_vev(0, emit_vproj(0, None, None, ekh))
            e_vev = e_vev_h[h]
            e_kev, e_kdup = e_kev_h[h], e_kdup_h[h]
            jobs = []
            for kt in range(NT):
                nq = T - kt * 128
                for l0 in range(0, nq, 512):
                    jobs.append((kt, l0, min(l0 + 512, nq)))
            nj = len(jobs)
            e_exp = [None] * nj
            e_pv = [None] * nj
            e_smm = [None] * nj
            e_msk = [None] * nj

            def emit_S(j):
                kt, l0, l1 = jobs[j]
                jb = j % 4
                rb = ((j + kr // 64) % 2) * 64
                if j < 4:
                    if h >= 1:
                        w("T", self.e_exp_last[h - 1])
                    w("T", e_kev if rb == kr else e_kdup)
                    w("T", e_qev[h][1] if rb == 64 else e_qev[h][0])
                else:
                    w("T", e_exp[j - 4])
                mm = TN.matmul(
                    self.pA[:, jb * 512:jb * 512 + (l1 - l0)],
                    self.k_sb[rb:rb + 64, kt * 128:(kt + 1) * 128],
                    self.q_sb[rb:rb + 64, kt * 128 + l0:kt * 128 + l1],
                    start=True, stop=True, skip_group_check=True)
                e_smm[j] = inc(mm, "tT")

            def emit_exp_quad(p):
                j0, j3 = 2 * p, 2 * p + 3
                jp = j0 % 8
                w("S", e_smm[j3], self.ld_qksc)
                if j0 < 8:
                    if h >= 1:
                        w("S", e_pv_last[h - 1])
                else:
                    w("S", e_pv[j3 - 8])
                wtot = 1536 + (jobs[j3][2] - jobs[j3][1])
                tok = inc(SN.activation(
                    self.pt_sb[:, jp * 512:jp * 512 + wtot],
                    self.pA[:, 0:wtot],
                    AF.Exp, bias=self.qksc[:, 0:1]), "tS")
                for i in range(4):
                    e_exp[j0 + i] = tok
                    if jobs[j0 + i][1] == 0:
                        w("V", tok, self.ld_tri)
                        e_msk[j0 + i] = inc(VN.tensor_tensor(
                            self.pt_sb[:, (jp + i) * 512:(jp + i) * 512 + 128],
                            self.pt_sb[:, (jp + i) * 512:(jp + i) * 512 + 128],
                            self.tri16[:], OP.mult), "tV")

            def emit_PV(j):
                kt, l0, l1 = jobs[j]
                jb = j % 8
                w("T", e_msk[j] if l0 == 0 else e_exp[j])
                if j == 0:
                    w("T", e_vev)
                    if h >= 1:
                        w("T", self.e_ynorm[h - 1])
                g_s = kt * 128 + l0
                g_e = kt * 128 + l1
                for ci in range(g_s // 512, (g_e + 511) // 512):
                    g0 = max(g_s, ci * 512)
                    g1 = min(g_e, (ci + 1) * 512)
                    rhs = self.pt_sb[:, jb * 512 + g0 - g_s:jb * 512 + g1 - g_s]
                    last = (j == nj - 1 and g1 == g_e)
                    if lo:
                        mm = TN.matmul(self.pB[0:65, g0:g1], self.v_lo[:, kt, :],
                                       rhs, start=(kt == 0), stop=last, skip_group_check=True)
                    else:
                        TN.matmul(self.pB[64:128, g0:g1], self.v_tmp[:, kt, :],
                                  rhs, start=(kt == 0), stop=last, skip_group_check=True,
                                  tile_position=(0, 64))
                        mm = TN.matmul(self.pB[0:64, g0:g1], self.onesb[:, 0:64],
                                       rhs, start=(kt == 0), stop=last, skip_group_check=True,
                                       tile_position=(0, 0))
                e_pv[j] = inc(mm, "tT")

            j_mid = max(jj for jj, (kt, l0, l1) in enumerate(jobs) if kt * 128 + l0 < 1024)
            p_mid = j_mid // 2 + 1
            if not EARLY_EPI:
                p_mid = 10 ** 9

            sp = 64 if lo else 0
            yr0, yr1 = (0, 64) if lo else (64, 128)
            lnrow = self.rsbc[sp:sp + 1, :]
            pbc = self.pB[yr0:yr1, 0:1024]
            e_l2 = [None, None]
            e_ln = [None, None]
            e_yev = [None, None]
            e_pb = [None, None]
            e_yn = [None, None]
            r16 = [None, None]

            def emit_epi_scalar(hf, pv_tok):
                hc = hf * 1024
                rp = sp if hf == 0 else 32
                r16[hf] = self.r16b[rp:rp + 1, :]
                psrow = (self.pB[64:65, hc:hc + 1024] if lo
                         else self.pB[0:1, hc:hc + 1024])
                w("S", pv_tok)
                if h >= 1:
                    w("S", self.e_ynorm[h - 1])
                if hf == 1:
                    w("S", e_l2[0])
                e_ln[hf] = inc(SN.activation(lnrow, psrow, AF.Ln), "tS")
                w("S", e_ln[hf], self.ld_qksc)
                if h >= 1:
                    w("S", e_pb_last[h - 1])
                e_l2[hf] = inc(SN.activation(r16[hf], lnrow, AF.Exp,
                                             scale=self.qksc[sp:sp + 1, 1:2]), "tS")
                w("V", pv_tok)
                if h >= 2:
                    w("V", self.e_ynorm[h - 2])
                e_yev[hf] = inc(VN.tensor_copy(
                    self.ytmp[yr0:yr1, hc:hc + 1024],
                    self.pB[yr0:yr1, hc:hc + 1024]), "tV")

            def emit_epi_bcast(hf):
                hc = hf * 1024
                rp = sp if hf == 0 else 32
                w("T", e_l2[hf], e_yev[hf])
                if hf == 1:
                    w("T", e_yn[0])
                for ck in range(2):
                    mm = TN.matmul(pbc[:, ck * 512:(ck + 1) * 512],
                                   self.onesb[rp:rp + 1, :],
                                   r16[hf][:, ck * 512:(ck + 1) * 512],
                                   start=True, stop=True)
                e_pb[hf] = inc(mm, "tT")
                w("V", e_pb[hf], e_yev[hf])
                ydst = self.y_fm[par * 64:(par + 1) * 64, ch, hc:hc + 1024]
                e_yn[hf] = inc(VN.tensor_tensor(
                    ydst, self.ytmp[yr0:yr1, hc:hc + 1024], pbc, OP.mult), "tV")

            for j in range(4):
                emit_S(j)
            for p in range(nj // 2):
                if p % 2 == 0:
                    emit_exp_quad(p)
                if 2 * p + 4 < nj:
                    emit_S(2 * p + 4)
                if 2 * p + 5 < nj:
                    emit_S(2 * p + 5)
                emit_PV(2 * p)
                emit_PV(2 * p + 1)
                if p == p_mid:
                    emit_epi_scalar(0, e_pv[j_mid])
                if p == p_mid + 1:
                    emit_epi_bcast(0)
            e_slast[h] = e_smm[nj - 1]
            self.e_exp_last[h] = e_exp[nj - 1]
            e_pv_last[h] = e_pv[nj - 1]
            if not EARLY_EPI:
                emit_epi_scalar(0, e_pv_last[h])
                emit_epi_bcast(0)
            emit_epi_scalar(1, e_pv_last[h])
            e_yev_last[h] = e_yev[1]
            if h + 1 < H:
                e_qn = emit_qproj(h + 1, [e_exp[nj - 4 + ck] for ck in range(4)])
                e_qev[h + 1] = emit_qdup(h + 1, e_qn)
                e_kev_h[h + 1], e_kdup_h[h + 1], ekh = emit_kchain(
                    h + 1, 64 - kr, None, e_ln[1])
            emit_epi_bcast(1)
            if h + 1 < H:
                e_vev_h[h + 1] = emit_vev(
                    h + 1, emit_vproj(h + 1, e_ln[1], e_yev[1], ekh))
            e_pb_last[h] = e_pb[1]
            self.e_rsev[h] = self.e_exp_last[h]
            self.e_ynorm[h] = e_yn[1]

    def emit_wu(self):
        nc, w, inc = self.nc, self.w, self.inc
        TN, VN = nc.tensor, nc.vector
        self.ld_wu = self.gload(self.wuT_sb[:],
                                self.d_wuT.rearrange("p (c o) -> p c o", c=DC))
        self.e_wuadd = [None] * NT
        e_badd = [None] * NT
        for tt in range(NT):
            w("V", self.e_bias)
            e_badd[tt] = inc(VN.tensor_tensor(self.acc[:, tt, :], self.acc[:, tt, :],
                                              self.bias_bc[:], OP.add), "tV")
        for tt in range(NT):
            w("T", self.e_ynorm[H - 1], self.ld_wu)
            if tt <= 1:
                w("T", self.e_rsev[H - 1])
            if tt >= 2:
                w("T", self.e_wuadd[tt - 2])
            for dc in range(DC):
                for c2 in range(2):
                    o0 = (tt % 2) * 1024 + c2 * 512
                    mm = TN.matmul(self.pA[:, o0:o0 + 512],
                                   self.y_fm[:, dc, tt * 128:(tt + 1) * 128],
                                   self.wuT_sb[:, dc, c2 * 512:(c2 + 1) * 512],
                                   start=(dc == 0), stop=(dc == DC - 1),
                                   skip_group_check=True)
            e_wu = inc(mm, "tT")
            w("V", e_wu, e_badd[tt])
            self.e_wuadd[tt] = inc(VN.tensor_tensor(
                self.acc[:, tt, :], self.acc[:, tt, :],
                self.pA[:, (tt % 2) * 1024:(tt % 2 + 1) * 1024], OP.add), "tV")
        if self.debug:
            w("Y", self.e_wuadd[NT - 1], self.e_ynorm[H - 1])
            self.sload(self.d_dbg_y, self.y_fm.ap().rearrange("p c t -> p (c t)"))
            self.e_dbgacc = self.sload(self.d_dbg_acc, self.acc.ap().rearrange("p c t -> p (c t)"))

    def emit_ffn(self):
        nc, w, inc = self.nc, self.w, self.inc
        TN, VN, SN, GN = nc.tensor, nc.vector, nc.scalar, nc.gpsimd
        self.gload(self.wiq[1][:], self.d_wiT[:, 8192:16384]
                   .rearrange("p (c f) -> p c f", c=DC))
        self.ld_w[1] = self.gload(self.woq[1][:], self.d_woT[:, 8192:16384]
                                  .rearrange("p (c f) -> p c f", c=8))
        ld_bcat = self.gload(self.bcat_sb[:], self.d_bcatT)
        e_relu = {}
        e_wo = {}
        e_add = {}
        e_st = []
        e_base = {}
        seq = [(q, tq) for q in range(4) for tq in range(4)]

        def emit_base(i):
            q, tq = seq[i]
            qb = q % 2
            ib = i % 2
            for fc in range(8):
                gfc = q * 8 + fc
                pbv = self.pB[:, (fc % 2) * 512:(fc % 2 + 1) * 512]
                w("T", self.ld_w[q], self.e_zm, ld_bcat)
                if i == 0 and fc < 2:
                    w("T", self.e_ynorm[H - 1])
                else:
                    prev = ((q, tq, fc - 2) if fc >= 2 else
                            (seq[i - 1][0], seq[i - 1][1], fc + 6))
                    w("T", e_relu[prev])
                for dc in range(DC):
                    TN.matmul(pbv, self.wiq[qb][:, dc, fc * 128:(fc + 1) * 128],
                              self.xh_fm[:, dc, tq * 512:(tq + 1) * 512],
                              start=(dc == 0), stop=False, skip_group_check=True)
                mm = TN.matmul(pbv, self.bcat_sb[:, gfc * 128:(gfc + 1) * 128],
                               self.zm_sb[:, tq * 512:(tq + 1) * 512],
                               start=False, stop=True, skip_group_check=True)
                e_base[(q, tq, fc)] = inc(mm, "tT")
                w("S", e_base[(q, tq, fc)], self.ld_wib)
                if i >= 2:
                    w("S", e_wo[(seq[i - 2][0], seq[i - 2][1], 3)])
                e_relu[(q, tq, fc)] = inc(
                    SN.activation(self.inter[ib][:, fc, :], pbv, AF.Relu,
                                  bias=self.wib_sb[:, gfc:gfc + 1]), "tS")

        def emit_wo(i):
            q, tq = seq[i]
            qb = q % 2
            ib = i % 2
            for t4 in range(4):
                tt = tq * 4 + t4
                w("T", e_relu[(q, tq, 7)])
                if q == 0:
                    w("T", self.e_wuadd[tt])
                else:
                    w("T", e_add[(q - 1, 14 + (tt % 2))])
                if (q, tt - 2) in e_add:
                    w("T", e_add[(q, tt - 2)])
                for fc in range(8):
                    for c2 in range(2):
                        o0 = (tt % 2) * 1024 + c2 * 512
                        mm = TN.matmul(self.pA[:, o0:o0 + 512],
                                       self.inter[ib][:, fc, t4 * 128:(t4 + 1) * 128],
                                       self.woq[qb][:, fc, c2 * 512:(c2 + 1) * 512],
                                       start=(fc == 0), stop=(fc == 7),
                                       skip_group_check=True)
                e_wo[(q, tq, t4)] = inc(mm, "tT")
                w("V", e_wo[(q, tq, t4)])
                if self.debug:
                    w("V", self.e_dbgacc)
                e_add[(q, tt)] = inc(VN.tensor_tensor(
                    self.acc[:, tt, :], self.acc[:, tt, :],
                    self.pA[:, (tt % 2) * 1024:(tt % 2 + 1) * 1024], OP.add), "tV")
                if q == 3:
                    w("Y", e_add[(q, tt)])
                    e_st.append(self.sload(self.d_out_t[tt], self.acc[:, tt, :]))

        emit_base(0)
        for i in range(16):
            if i + 1 < 16:
                emit_base(i + 1)
            emit_wo(i)
            q, tq = seq[i]
            if tq == 3 and q + 2 <= 3:
                w("G", e_base[(q, 3, 7)], e_wo[(q, 3, 3)])
                nq_ = q + 2
                self.gload(self.wiq[q % 2][:], self.d_wiT[:, nq_ * 8192:(nq_ + 1) * 8192]
                           .rearrange("p (c f) -> p c f", c=DC))
                self.ld_w[nq_] = self.gload(
                    self.woq[q % 2][:], self.d_woT[:, nq_ * 8192:(nq_ + 1) * 8192]
                    .rearrange("p (c f) -> p c f", c=8))
        w("Y", e_st[-1])


def build_nc(debug=False):
    kb = KB(debug=debug)
    return kb.build()


def host_inputs(L0_attn_Wq, L0_attn_Wk, L0_attn_Wv, L0_attn_Wu, L0_attn_bu,
                L0_ffn_gate_W, L0_ffn_gate_b, L0_ffn_wi_W, L0_ffn_wi_b,
                L0_ffn_wo_W, L0_ffn_wo_b, L0_ffn_lora_A, L0_ffn_lora_B):
    f32 = lambda a: np.ascontiguousarray(np.asarray(a, dtype=np.float32))
    Wq, Wk, Wv, Wu = f32(L0_attn_Wq), f32(L0_attn_Wk), f32(L0_attn_Wv), f32(L0_attn_Wu)
    gw, wi, wo = f32(L0_ffn_gate_W), f32(L0_ffn_wi_W), f32(L0_ffn_wo_W)
    la, lb = f32(L0_ffn_lora_A), f32(L0_ffn_lora_B)
    d = {}
    d["wqT"] = np.ascontiguousarray(Wq.T)
    d["wkT"] = np.ascontiguousarray(Wk.T)
    d["wvT"] = np.ascontiguousarray(Wv.T)
    d["wuT"] = np.ascontiguousarray(
        Wu.T.reshape(DC, 128, D).transpose(1, 0, 2).reshape(128, DC * D))
    d["bu"] = f32(L0_attn_bu).reshape(1, D)
    d["gwT"] = np.ascontiguousarray(
        gw.T.reshape(DC, 128, E).transpose(1, 0, 2).reshape(128, DC * E))
    d["gb"] = f32(L0_ffn_gate_b).reshape(E, 1)
    d["wiT"] = np.ascontiguousarray(
        wi.reshape(4, 1024, DC, 128).transpose(3, 0, 2, 1).reshape(128, 4 * DC * 1024))
    d["wib"] = np.ascontiguousarray(f32(L0_ffn_wi_b).reshape(FF // 128, 128).T)
    d["woT"] = np.ascontiguousarray(
        wo.T.reshape(4, 8, 128, 1024).transpose(2, 0, 1, 3).reshape(128, 4 * 8 * 1024))
    d["wob"] = f32(L0_ffn_wo_b).reshape(1, D)
    d["acatT"] = np.ascontiguousarray(
        la.reshape(E * R, D).T.reshape(DC, 128, E * R).transpose(1, 0, 2)
        .reshape(128, DC * E * R))
    d["bcatT"] = np.ascontiguousarray(lb.transpose(0, 2, 1).reshape(E * R, FF))
    p = np.arange(128)[:, None]
    f = np.arange(128)[None, :]
    d["cmaskT"] = np.where(f >= p, 0.0, NEG).astype(np.float32)
    d["ciota"] = np.broadcast_to(np.repeat(np.arange(E, dtype=np.float32), R),
                                 (128, E * R)).copy()
    d["cqksc"] = np.concatenate([np.full((128, 1), MSHIFT, np.float32),
                                 np.full((128, 1), -1.0, np.float32)], axis=1)
    import ml_dtypes
    d["cones32"] = np.ones((128, 128), dtype=np.float32)
    d["cones16"] = np.ones((128, NT * (C + 1)), dtype=ml_dtypes.bfloat16)
    d["cid16"] = np.eye(128).astype(ml_dtypes.bfloat16)
    d["ctri16"] = np.where(f < p, 0.0, 1.0).astype(ml_dtypes.bfloat16)
    d["cident"] = np.eye(128, dtype=np.float32)
    return d


def kernel(L0_attn_x, L0_attn_Wq, L0_attn_Wk, L0_attn_Wv, L0_attn_Wu, L0_attn_bu,
           L0_ffn_gate_W, L0_ffn_gate_b, L0_ffn_wi_W, L0_ffn_wi_b,
           L0_ffn_wo_W, L0_ffn_wo_b, L0_ffn_lora_A, L0_ffn_lora_B):
    x = np.ascontiguousarray(np.asarray(L0_attn_x, dtype=np.float32))
    shared = host_inputs(L0_attn_Wq, L0_attn_Wk, L0_attn_Wv, L0_attn_Wu, L0_attn_bu,
                         L0_ffn_gate_W, L0_ffn_gate_b, L0_ffn_wi_W, L0_ffn_wi_b,
                         L0_ffn_wo_W, L0_ffn_wo_b, L0_ffn_lora_A, L0_ffn_lora_B)
    nc = build_nc()
    in_maps = [dict(shared, x=np.ascontiguousarray(x[i])) for i in range(NCORES)]
    res = run_bass_kernel_spmd(nc, in_maps, core_ids=list(range(NCORES)))
    return np.stack([np.asarray(res.results[i]["out"], dtype=np.float32)
                     for i in range(NCORES)], axis=0)


# version 30
# speedup vs baseline: 1.2168x; 1.2168x over previous
import contextlib
import numpy as np

import concourse.bass as bass
import concourse.mybir as mybir
from concourse.bass_utils import run_bass_kernel_spmd

F32 = mybir.dt.float32
BF16 = mybir.dt.bfloat16
U32 = mybir.dt.uint32
AF = mybir.ActivationFunctionType
OP = mybir.AluOpType
AX = mybir.AxisListType

T = 2048
D = 1024
H = 16
C = 64
FF = 4096
E = 8
R = 4
NT = T // 128
DC = D // 128
NCORES = 8
EPS = 1e-5
MSHIFT = -20.0
NEG = -1.0e30
EARLY_EPI = True


class KB:
    def __init__(self, debug=False):
        self.debug = debug
        self.nc = bass.Bass("TRN2", target_bir_lowering=False)
        self.cnt = {}
        self.seen = {e: {} for e in ("T", "V", "S", "G", "Y")}
        self.sems = {}

    def inc(self, instr, sem, n=1):
        self.cnt[sem] += n
        instr.then_inc(self.sems[sem], n)
        return (sem, self.cnt[sem])

    def w(self, eng, *toks):
        for tok in toks:
            if tok is None:
                continue
            sem, val = tok
            if val <= 0:
                continue
            assert val <= self.cnt[sem], f"wait on future {sem}={val}>{self.cnt[sem]}"
            s = self.seen[eng]
            if s.get(sem, 0) >= val:
                continue
            self.eng[eng].wait_ge(self.sems[sem], val)
            s[sem] = val

    def gload(self, dst, src):
        self.w("G", ("dG", self.cnt["dG"]))
        return self.inc(self.nc.gpsimd.dma_start(out=dst, in_=src), "dG", 16)

    def sload(self, dst, src):
        self.w("Y", ("dS", self.cnt["dS"]))
        return self.inc(self.nc.sync.dma_start(out=dst, in_=src), "dS", 16)

    def stransp(self, dst, src):
        self.w("Y", ("dS", self.cnt["dS"]))
        return self.inc(self.nc.sync.dma_start(out=dst, in_=src, transpose=True),
                        "dS", 16)

    def xload(self, dst, src, g):
        return self.inc(self.nc.sync.dma_start(out=dst, in_=src),
                        f"dX{g}", 16)

    def tload(self, dst, src, g):
        return self.inc(self.nc.sync.dma_start(out=dst, in_=src, transpose=True),
                        f"dU{g}", 16)

    def build(self):
        nc = self.nc
        es = contextlib.ExitStack()
        A = es.enter_context
        self.eng = {"T": nc.tensor, "V": nc.vector, "S": nc.scalar,
                    "G": nc.gpsimd, "Y": nc.sync}
        for name in (["dG", "dS", "tT", "tV", "tS", "tG"]
                     + [f"dX{i}" for i in range(4)]
                     + [f"dU{i}" for i in range(8)]):
            self.sems[name] = A(nc.semaphore(name))
            self.cnt[name] = 0

        self.d_x = nc.dram_tensor("x", [T, D], F32, kind="ExternalInput").ap()
        self.d_wqT = nc.dram_tensor("wqT", [C, C], F32, kind="ExternalInput").ap()
        self.d_wkT = nc.dram_tensor("wkT", [C, C], F32, kind="ExternalInput").ap()
        self.d_wvT = nc.dram_tensor("wvT", [C, C], F32, kind="ExternalInput").ap()
        self.d_wuT = nc.dram_tensor("wuT", [128, DC * D], F32, kind="ExternalInput").ap()
        self.d_bu = nc.dram_tensor("bu", [1, D], F32, kind="ExternalInput").ap()
        self.d_gwT = nc.dram_tensor("gwT", [128, DC * E], F32, kind="ExternalInput").ap()
        self.d_gb = nc.dram_tensor("gb", [E, 1], F32, kind="ExternalInput").ap()
        self.d_wiT = nc.dram_tensor("wiT", [128, 4 * DC * 1024], F32, kind="ExternalInput").ap()
        self.d_wib = nc.dram_tensor("wib", [128, FF // 128], F32, kind="ExternalInput").ap()
        self.d_woT = nc.dram_tensor("woT", [128, 4 * 8 * 1024], F32, kind="ExternalInput").ap()
        self.d_wob = nc.dram_tensor("wob", [1, D], F32, kind="ExternalInput").ap()
        self.d_acatT = nc.dram_tensor("acatT", [128, DC * E * R], F32, kind="ExternalInput").ap()
        self.d_bcatT = nc.dram_tensor("bcatT", [E * R, FF], F32, kind="ExternalInput").ap()
        self.d_maskT = nc.dram_tensor("cmaskT", [128, 128], F32, kind="ExternalInput").ap()
        self.d_iota = nc.dram_tensor("ciota", [128, E * R], F32, kind="ExternalInput").ap()
        self.d_qksc = nc.dram_tensor("cqksc", [128, 2], F32, kind="ExternalInput").ap()
        self.d_ident = nc.dram_tensor("cident", [128, 128], F32, kind="ExternalInput").ap()
        self.d_ones32 = nc.dram_tensor("cones32", [128, 128], F32, kind="ExternalInput").ap()
        self.d_id16 = nc.dram_tensor("cid16", [128, 128], BF16, kind="ExternalInput").ap()
        self.d_tri16 = nc.dram_tensor("ctri16", [128, 128], BF16, kind="ExternalInput").ap()
        self.d_ones16 = nc.dram_tensor("cones16", [128, NT * (C + 1)], BF16, kind="ExternalInput").ap()
        d_out = nc.dram_tensor("out", [T, D], F32, kind="ExternalOutput").ap()
        if self.debug:
            self.d_dbg_xh = nc.dram_tensor("dbg_xh", [128, DC * T], BF16, kind="ExternalOutput").ap()
            self.d_dbg_lg = nc.dram_tensor("dbg_lg", [E, T], F32, kind="ExternalOutput").ap()
            self.d_dbg_zm = nc.dram_tensor("dbg_zm", [E * R, T], BF16, kind="ExternalOutput").ap()
            self.d_dbg_y = nc.dram_tensor("dbg_y", [128, DC * T], BF16, kind="ExternalOutput").ap()
            self.d_dbg_acc = nc.dram_tensor("dbg_acc", [128, NT * D], F32, kind="ExternalOutput").ap()
            self.d_dbg_q = nc.dram_tensor("dbg_q", [64, T], BF16, kind="ExternalOutput").ap()
            self.d_dbg_k = nc.dram_tensor("dbg_k", [64, T], BF16, kind="ExternalOutput").ap()
            self.d_dbg_rs = nc.dram_tensor("dbg_rs", [65, 1024], F32, kind="ExternalOutput").ap()
        self.d_x_t = self.d_x.rearrange("(n p) m -> n p m", p=128)
        self.d_out_t = d_out.rearrange("(n p) m -> n p m", p=128)

        self.acc = A(nc.sbuf_tensor("acc", [128, NT, D], F32))
        self.bias_bc = A(nc.sbuf_tensor("bias_bc", [128, D], F32))
        self.xh_fm = A(nc.sbuf_tensor("xh_fm", [128, DC, T], BF16))
        self.wiq = [A(nc.sbuf_tensor("wiq0", [128, DC, 1024], BF16)), None]
        self.woq = [A(nc.sbuf_tensor("woq0", [128, 8, 1024], BF16)), None]
        self.zm_sb = A(nc.sbuf_tensor("zm_sb", [E * R, T], BF16))
        self.wq_sb = A(nc.sbuf_tensor("wq_sb", [128, C], BF16))
        self.wk_sb = A(nc.sbuf_tensor("wk_sb", [128, C], BF16))
        self.wv_sb = A(nc.sbuf_tensor("wv_sb", [128, C], BF16))
        self.gwh = A(nc.sbuf_tensor("gwh", [128, DC, E], BF16))
        self.gwl = A(nc.sbuf_tensor("gwl", [128, DC, E], BF16))
        self.gw32 = A(nc.sbuf_tensor("gw32", [128, DC, E], F32))
        self.gb_sb = A(nc.sbuf_tensor("gb_sb", [E, 1], F32))
        self.acat_sb = A(nc.sbuf_tensor("acat_sb", [128, DC, E * R], BF16))
        self.wib_sb = A(nc.sbuf_tensor("wib_sb", [128, FF // 128], F32))
        self.maskT = A(nc.sbuf_tensor("maskT_sb", [128, 128], F32))
        self.qksc = A(nc.sbuf_tensor("qksc_sb", [128, 2], F32))
        self.ident = A(nc.sbuf_tensor("ident_sb", [128, 128], F32))
        self.onesf = A(nc.sbuf_tensor("onesf", [128, 128], F32))
        self.id16 = A(nc.sbuf_tensor("id16_sb", [128, 128], BF16))
        self.tri16 = A(nc.sbuf_tensor("tri16_sb", [128, 128], BF16))

        self.v_lo = A(nc.sbuf_tensor("v_lo", [128, NT, C + 1], BF16))
        self.v_tmp = A(nc.sbuf_tensor("v_tmp", [128, NT, C], BF16))
        self.onesb = A(nc.sbuf_tensor("onesb", [128, 64], BF16))

        self.pA = A(nc.psum_tensor("pA", [128, 2048], F32))
        self.pB = A(nc.psum_tensor("pB", [128, 2048], F32))

        with contextlib.ExitStack() as es1:
            A1 = es1.enter_context
            self.sq_sc = A1(nc.sbuf_tensor("sq_sc", [128, D], F32))
            self.sv = A1(nc.sbuf_tensor("sv", [128, 6, NT], F32))
            self.st1 = A1(nc.sbuf_tensor("st1", [128, NT], F32))
            self.st2 = A1(nc.sbuf_tensor("st2", [128, NT], F32))
            self.max8 = A1(nc.sbuf_tensor("max8", [128, 8], F32))
            self.idx8 = A1(nc.sbuf_tensor("idx8", [128, 8], U32))
            self.idxf = A1(nc.sbuf_tensor("idxf", [128, 1], F32))
            self.lgt_tm = A1(nc.sbuf_tensor("lgt_tm", [128, NT, E], F32))
            self.oh32 = A1(nc.sbuf_tensor("oh32", [128, NT, E * R], F32))
            self.iota32 = A1(nc.sbuf_tensor("iota32_sb", [128, E * R], F32))
            self.xn32 = [A1(nc.sbuf_tensor(f"xn32_{i}", [128, D], F32)) for i in range(2)]
            self.xn16 = [A1(nc.sbuf_tensor(f"xn16_{i}", [128, D], BF16)) for i in range(2)]
            self.xl16 = [A1(nc.sbuf_tensor(f"xl16_{i}", [128, D], BF16)) for i in range(2)]
            self.xlfm = [A1(nc.sbuf_tensor(f"xlfm_{i}", [128, DC, 128], BF16)) for i in range(2)]
            self.logits_sb = A1(nc.sbuf_tensor("logits_sb", [E, T], F32))
            self.mz_sb = A1(nc.sbuf_tensor("mz_sb", [E * R, T], BF16))
            with nc.Block() as blk1:
                @blk1.sync
                def _(_sync):
                    self.emit_p0()
                    self.emit_p1()
                    self.emit_p15()
        with contextlib.ExitStack() as esy:
            self.y_fm = esy.enter_context(nc.sbuf_tensor("y_fm", [128, DC, T], BF16))
            with contextlib.ExitStack() as es2:
                A2 = es2.enter_context
                self.q_sb = A2(nc.sbuf_tensor("q_sb", [128, T], BF16))
                self.k_sb = A2(nc.sbuf_tensor("k_sb", [128, T], BF16))
                self.pt_sb = A2(nc.sbuf_tensor("pt_sb", [128, 6 * 512], BF16))
                self.rsbc = A2(nc.sbuf_tensor("rsbc", [128, 1024], F32))
                self.ytmp = A2(nc.sbuf_tensor("ytmp", [128, 2048], BF16))
                self.r16b = A2(nc.sbuf_tensor("r16b", [128, 1024], BF16))
                with nc.Block() as blk2:
                    @blk2.sync
                    def _(_sync):
                        self.emit_att()
            with contextlib.ExitStack() as es3a:
                self.wuT_sb = es3a.enter_context(nc.sbuf_tensor("wuT_sb", [128, DC, D], BF16))
                with nc.Block() as blk3:
                    @blk3.sync
                    def _(_sync):
                        self.emit_wu()
        with contextlib.ExitStack() as es3:
            A3 = es3.enter_context
            self.wiq[1] = A3(nc.sbuf_tensor("wiq1", [128, DC, 1024], BF16))
            self.woq[1] = A3(nc.sbuf_tensor("woq1", [128, 8, 1024], BF16))
            self.bcat_sb = A3(nc.sbuf_tensor("bcat_sb", [E * R, FF], BF16))
            self.inter = [A3(nc.sbuf_tensor(f"inter{i}", [128, 8, 512], BF16)) for i in range(2)]
            with nc.Block() as blk4:
                @blk4.sync
                def _(_sync):
                    self.emit_ffn()

        es.close()
        return nc

    def emit_p0(self):
        nc, w, inc = self.nc, self.w, self.inc
        TN, VN, SN, GN = nc.tensor, nc.vector, nc.scalar, nc.gpsimd
        self.ld_x = []
        for wv in range(4):
            for i in range(4):
                tok = self.xload(self.acc[:, wv * 4 + i, :],
                                 self.d_x_t[wv * 4 + i], wv)
            self.ld_x += [tok] * 4

        self.burow = self.logits_sb[0:1, 0:D]
        self.wobrow = self.logits_sb[0:1, D:2 * D]
        self.brow = self.sq_sc[0:1, 0:D]

        self.gload(self.onesf[:], self.d_ones32)
        self.gload(self.onesb[:], self.d_ones16[:, 0:64])
        self.gload(self.id16[:], self.d_id16)
        self.ld_tri = self.gload(self.tri16[:], self.d_tri16)
        self.e_memset = self.gload(self.v_lo.ap().rearrange("p k c -> p (k c)"),
                                   self.d_ones16)
        ld_gw32 = self.gload(self.gw32[:], self.d_gwT.rearrange("p (c e) -> p c e", c=DC))
        ld_gwh = self.gload(self.gwh[:], self.d_gwT.rearrange("p (c e) -> p c e", c=DC))
        ld_bu = self.gload(self.burow, self.d_bu)
        ld_wob = self.gload(self.wobrow, self.d_wob)
        self.ld_iota = self.gload(self.iota32[:], self.d_iota)
        self.ld_qksc = self.gload(self.qksc[:], self.d_qksc)
        self.ld_ident = self.gload(self.ident[:], self.d_ident)
        self.gload(self.wq_sb[0:64, :], self.d_wqT)
        self.gload(self.wk_sb[0:64, :], self.d_wkT)
        self.gload(self.wv_sb[0:64, :], self.d_wvT)
        self.ld_wq = self.gload(self.wq_sb[64:128, :], self.d_wqT)
        self.ld_wk = self.gload(self.wk_sb[64:128, :], self.d_wkT)
        self.ld_wv = self.gload(self.wv_sb[64:128, :], self.d_wvT)
        self.ld_gb = self.gload(self.gb_sb[:], self.d_gb)
        self.ld_wib = self.gload(self.wib_sb[:], self.d_wib)
        self.ld_acat = self.gload(self.acat_sb[:],
                                  self.d_acatT.rearrange("p (c j) -> p c j", c=DC))
        self.ld_w = [None] * 4
        self.gload(self.wiq[0][:], self.d_wiT[:, 0:8192].rearrange("p (c f) -> p c f", c=DC))
        self.ld_w[0] = self.gload(self.woq[0][:],
                                  self.d_woT[:, 0:8192].rearrange("p (c f) -> p c f", c=8))
        self._p0v = (ld_gw32, ld_gwh, ld_bu, ld_wob)

    def emit_p1(self):
        nc, w, inc = self.nc, self.w, self.inc
        TN, VN, SN = nc.tensor, nc.vector, nc.scalar
        sv, st1, st2 = self.sv, self.st1, self.st2
        self.e_gate = [None] * NT
        self.e_trn = [None] * NT
        self.e_trl = [None] * NT
        e_xlv = [None] * NT
        e_sq = [None] * NT
        for tt in range(NT):
            w("V", self.ld_x[tt])
            inc(VN.reduce_sum(st1[:, tt:tt + 1], self.acc[:, tt, :], axis=AX.X), "tV")
            if tt == 8:
                ld_gw32, ld_gwh, ld_bu, ld_wob = self._p0v
                w("V", ld_gw32, ld_gwh)
                self.e_gwl = inc(VN.tensor_tensor(self.gwl[:], self.gw32[:],
                                                  self.gwh[:], OP.subtract), "tV")
                w("V", ld_bu, ld_wob)
                e_brow = inc(VN.tensor_tensor(self.brow, self.burow, self.wobrow,
                                              OP.add), "tV")
                w("T", e_brow)
                TN.matmul(self.pA[:, 0:512], self.onesf[0:1, :], self.brow[:, 0:512],
                          start=True, stop=True)
                self.e_bmm = inc(TN.matmul(self.pA[:, 512:1024], self.onesf[0:1, :],
                                           self.brow[:, 512:1024],
                                           start=True, stop=True), "tT")
                w("V", self.e_bmm)
                self.e_bias = inc(VN.tensor_copy(self.bias_bc[:], self.pA[:, 0:1024]),
                                  "tV")
            w("S", self.ld_x[tt])
            if tt >= 1:
                w("S", e_sq[tt - 1])
            e_sq[tt] = inc(SN.activation(self.xn32[0][:], self.acc[:, tt, :], AF.Square,
                                         accum_out=st2[:, tt:tt + 1]), "tS")
        e_sum = ("tV", self.cnt["tV"])
        w("V", e_sum, e_sq[NT - 1])
        e1 = inc(VN.tensor_scalar(sv[:, 0, :], st1[:], 1.0 / D, None, OP.mult), "tV")
        w("V", e1)
        e2 = inc(VN.tensor_tensor(sv[:, 1, :], sv[:, 0, :], sv[:, 0, :], OP.mult), "tV")
        w("V", e2)
        e3 = inc(VN.scalar_tensor_tensor(sv[:, 1, :], st2[:], 1.0 / D,
                                         sv[:, 1, :], OP.mult, OP.subtract), "tV")
        w("V", e3)
        e4 = inc(VN.tensor_scalar(sv[:, 1, :], sv[:, 1, :], EPS, None, OP.add), "tV")
        w("S", e4)
        e5 = inc(SN.activation(sv[:, 2, :], sv[:, 1, :], AF.Sqrt), "tS")
        w("V", e5)
        e6 = inc(VN.reciprocal(sv[:, 3, :], sv[:, 2, :]), "tV")
        w("V", e6)
        e7 = inc(VN.tensor_tensor(sv[:, 4, :], sv[:, 3, :], sv[:, 3, :], OP.mult), "tV")
        w("V", e7)
        e8 = inc(VN.tensor_tensor(sv[:, 4, :], sv[:, 4, :], sv[:, 1, :], OP.mult), "tV")
        w("V", e8)
        e9 = inc(VN.tensor_scalar(sv[:, 4, :], sv[:, 4, :], -0.5, 1.5, OP.mult, OP.add), "tV")
        w("V", e9)
        e10 = inc(VN.tensor_tensor(sv[:, 2, :], sv[:, 3, :], sv[:, 4, :], OP.mult), "tV")
        w("V", e10)
        e11 = inc(VN.scalar_tensor_tensor(sv[:, 5, :], sv[:, 0, :], -1.0,
                                          sv[:, 2, :], OP.mult, OP.mult), "tV")
        for tt in range(NT):
            bu2 = tt % 2
            wv = tt // 2
            w("S", e11)
            if tt >= 2:
                w("S", e_xlv[tt - 2])
            e_xn32 = inc(SN.activation(self.xn32[bu2][:], self.acc[:, tt, :], AF.Identity,
                                       bias=sv[:, 5, tt:tt + 1], scale=sv[:, 2, tt:tt + 1]), "tS")
            w("S", e_xn32)
            if tt >= 2:
                w("S", self.e_trn[tt - 2])
            e_x16 = inc(SN.activation(self.xn16[bu2][:], self.xn32[bu2][:],
                                      AF.Identity), "tS")
            w("V", e_x16)
            if tt >= 2:
                w("V", self.e_trl[tt - 2])
            e_xlv[tt] = inc(VN.tensor_tensor(self.xl16[bu2][:], self.xn32[bu2][:],
                                             self.xn16[bu2][:], OP.subtract), "tV")
            w("Y", e_xlv[tt], e_x16)
            if tt >= 2:
                w("Y", self.e_gate[tt - 2])
            self.tload(self.xh_fm[:, :, tt * 128:(tt + 1) * 128],
                       self.xn16[bu2].ap().rearrange("p (c f) -> p c f", c=DC), wv)
            tok = self.tload(
                self.xlfm[bu2][:], self.xl16[bu2].ap().rearrange("p (c f) -> p c f", c=DC), wv)
            if bu2 == 0:
                continue
            self.e_trn[tt - 1] = self.e_trn[tt] = tok
            self.e_trl[tt - 1] = self.e_trl[tt] = tok
            for t2 in (tt - 1, tt):
                b2 = t2 % 2
                w("T", tok, self.e_gwl)
                first = True
                for (xs, gs) in ((None, self.gwh), (None, self.gwl), (self.xlfm[b2], self.gwh)):
                    for dc in range(DC):
                        rhs = (self.xh_fm[:, dc, t2 * 128:(t2 + 1) * 128] if xs is None
                               else xs[:, dc, :])
                        mm = TN.matmul(self.pB[0:E, t2 * 128:(t2 + 1) * 128], gs[:, dc, :], rhs,
                                       start=first, stop=(dc == DC - 1 and xs is not None),
                                       skip_group_check=True)
                        first = False
                self.e_gate[t2] = inc(mm, "tT")

    def emit_p15(self):
        nc, w, inc = self.nc, self.w, self.inc
        TN, VN, SN = nc.tensor, nc.vector, nc.scalar
        w("S", self.e_gate[NT - 1], self.ld_gb)
        e_lg = inc(SN.activation(self.logits_sb[:], self.pB[0:E, :], AF.Identity,
                                 bias=self.gb_sb[:]), "tS")
        w("T", e_lg, self.ld_ident, self.e_bias)
        for tt in range(NT):
            mm = TN.transpose(self.pA[:, tt * E:(tt + 1) * E],
                              self.logits_sb[0:E, tt * 128:(tt + 1) * 128],
                              self.ident[0:E, 0:E])
        e_lgt = inc(mm, "tT")
        w("V", e_lgt)
        e_lgtv = inc(VN.tensor_copy(
            self.lgt_tm[:], self.pA[:, 0:NT * E].rearrange("p (n e) -> p n e", n=NT)), "tV")
        e_oh = [None] * NT
        for tt in range(NT):
            w("V", e_lgtv, self.ld_iota)
            ea = inc(VN.max(self.max8[:], self.lgt_tm[:, tt, :]), "tV")
            w("V", ea)
            eb = inc(VN.max_index(self.idx8[:], self.max8[:], self.lgt_tm[:, tt, :]), "tV")
            w("V", eb)
            ec = inc(VN.tensor_copy(self.idxf[:], self.idx8[:, 0:1]), "tV")
            w("V", ec)
            e_oh[tt] = inc(VN.tensor_scalar(self.oh32[:, tt, :], self.iota32[:],
                                            self.idxf[:], None, OP.is_equal), "tV")
        w("T", e_lg)
        for tt in range(NT):
            w("T", e_oh[tt])
            mm = TN.transpose(self.pB[0:E * R, tt * 128:(tt + 1) * 128],
                              self.oh32[:, tt, :], self.ident[:])
        e_mzt = inc(mm, "tT")
        w("V", e_mzt)
        e_mz = inc(VN.tensor_copy(self.mz_sb[:], self.pB[0:E * R, :]), "tV")
        w("T", self.ld_acat, e_lgtv, self.e_trn[NT - 1], self.e_trl[NT - 1])
        for dc in range(DC):
            for ck in range(4):
                mm = TN.matmul(self.pA[0:E * R, ck * 512:(ck + 1) * 512],
                               self.acat_sb[:, dc, :],
                               self.xh_fm[:, dc, ck * 512:(ck + 1) * 512],
                               start=(dc == 0), stop=(dc == DC - 1), skip_group_check=True)
        e_z = inc(mm, "tT")
        w("V", e_z, e_mz)
        self.e_zm = inc(VN.tensor_tensor(self.zm_sb[:], self.pA[0:E * R, :],
                                         self.mz_sb[:], OP.mult), "tV")
        if self.debug:
            w("Y", self.e_zm, e_lg, self.e_trn[NT - 1], self.e_trl[NT - 1])
            self.sload(self.d_dbg_xh, self.xh_fm.ap().rearrange("p c t -> p (c t)"))
            self.sload(self.d_dbg_lg, self.logits_sb[:])
            self.sload(self.d_dbg_zm, self.zm_sb[:])

    def emit_att(self):
        nc, w, inc = self.nc, self.w, self.inc
        TN, VN, SN, GN = nc.tensor, nc.vector, nc.scalar, nc.gpsimd
        self.e_ynorm = [None] * H
        self.e_rsev = [None] * H
        self.e_exp_last = [None] * H
        e_pv_last = [None] * H
        e_pb_last = [None] * H
        e_yev_last = [None] * H
        e_qev = [None] * H
        e_slast = [None] * H

        def hsl(hh):
            return self.xh_fm[(hh & 1) * 64:((hh & 1) + 1) * 64, hh >> 1, :]

        def emit_qproj(hh, e4):
            kq = (hh & 1) * 64
            wqh = self.wq_sb[kq:kq + 64, :]
            w("T", self.ld_wq, self.e_trn[NT - 1])
            for ck in range(4):
                if e4 is not None:
                    w("T", e4[ck])
                mmq = TN.matmul(self.pA[kq:kq + 64, ck * 512:(ck + 1) * 512], wqh,
                                hsl(hh)[:, ck * 512:(ck + 1) * 512], start=True, stop=True)
            return inc(mmq, "tT")

        def emit_qdup(hh, e_q):
            kq = (hh & 1) * 64
            w("V", e_q)
            ev = inc(VN.tensor_copy(self.q_sb[kq:kq + 64, :],
                                    self.pA[kq:kq + 64, :]), "tV")
            w("Y", ev)
            if hh >= 1:
                w("Y", e_slast[hh - 1])
            w("Y", ("dS", self.cnt["dS"]))
            ed = inc(nc.sync.dma_start(out=self.q_sb[64 - kq:128 - kq, :],
                                       in_=self.q_sb[kq:kq + 64, :]), "dS", 16)
            return ev, ed

        def emit_vproj(hh, e_ln1, e_yev1, e_kev2):
            w("T", self.ld_wv, self.e_trn[NT - 1])
            wvh = self.wv_sb[(hh & 1) * 64:((hh & 1) + 1) * 64, :]
            if hh >= 1:
                w("T", e_yev1)
                w("T", e_ln1)
            w("T", e_kev2)
            for kt in range(NT):
                mm = TN.matmul(self.pB[:, 1024 + kt * 64:1024 + (kt + 1) * 64],
                               hsl(hh)[:, kt * 128:(kt + 1) * 128], wvh,
                               start=True, stop=True)
            return inc(mm, "tT")

        def emit_vev(hh, e_vmm):
            w("V", e_vmm)
            if hh >= 2:
                w("V", e_pv_last[hh - 2])
            vdst = self.v_lo[:, :, 0:C] if (hh & 1) == 0 else self.v_tmp[:]
            return inc(VN.tensor_copy(
                vdst, self.pB[:, 1024:2048].rearrange("p (k c) -> p k c", k=NT)), "tV")

        def emit_kchain(hh, kr, e_vev_tok, e_ln1):
            wkh = self.wk_sb[(hh & 1) * 64:((hh & 1) + 1) * 64, :]
            hs2 = hsl(hh)
            w("T", self.ld_wk, e_vev_tok, e_ln1)
            for ck in range(4):
                mm = TN.matmul(self.pB[kr:kr + 64, ck * 512:(ck + 1) * 512], wkh,
                               hs2[:, ck * 512:(ck + 1) * 512], start=True, stop=True)
            e_qk = inc(mm, "tT")
            w("V", e_qk)
            e_kev_hi = inc(VN.tensor_scalar(self.k_sb[kr:kr + 64, 1024:2048],
                                            self.pB[kr:kr + 64, 1024:2048],
                                            1.0 / 32.0, None, OP.mult), "tV")
            e_kev = inc(VN.tensor_scalar(self.k_sb[kr:kr + 64, 0:1024],
                                         self.pB[kr:kr + 64, 0:1024], 1.0 / 32.0,
                                         None, OP.mult), "tV")
            w("Y", e_kev)
            if hh >= 1:
                w("Y", e_slast[hh - 1])
            w("Y", ("dS", self.cnt["dS"]))
            e_kdup = inc(nc.sync.dma_start(out=self.k_sb[64 - kr:128 - kr, :],
                                           in_=self.k_sb[kr:kr + 64, :]), "dS", 16)
            return e_kev, e_kdup, e_kev_hi

        e_vev_h = [None] * H
        e_kev_h = [None] * H
        e_kdup_h = [None] * H
        e_qn = emit_qproj(0, None)
        e_qev[0] = emit_qdup(0, e_qn)
        for h in range(H):
            par = h & 1
            ch = h >> 1
            lo = (par == 0)
            hs = hsl(h)
            kr = par * 64
            if h == 0:
                e_kev_h[0], e_kdup_h[0], ekh = emit_kchain(0, 0, None, None)
                e_vev_h[0] = emit_vev(0, emit_vproj(0, None, None, ekh))
            e_vev = e_vev_h[h]
            e_kev, e_kdup = e_kev_h[h], e_kdup_h[h]
            jobs = []
            for kt in range(NT):
                nq = T - kt * 128
                for l0 in range(0, nq, 512):
                    jobs.append((kt, l0, min(l0 + 512, nq)))
            nj = len(jobs)
            e_exp = [None] * nj
            e_pv = [None] * nj
            e_smm = [None] * nj
            e_msk = [None] * nj

            def emit_S(j):
                kt, l0, l1 = jobs[j]
                jb = j % 4
                rb = ((j + kr // 64) % 2) * 64
                if j < 4:
                    if h >= 1:
                        w("T", self.e_exp_last[h - 1])
                    w("T", e_kev if rb == kr else e_kdup)
                    w("T", e_qev[h][0] if rb == kr else e_qev[h][1])
                else:
                    w("T", e_exp[j - 4])
                mm = TN.matmul(
                    self.pA[:, jb * 512:jb * 512 + (l1 - l0)],
                    self.k_sb[rb:rb + 64, kt * 128:(kt + 1) * 128],
                    self.q_sb[rb:rb + 64, kt * 128 + l0:kt * 128 + l1],
                    start=True, stop=True, skip_group_check=True)
                e_smm[j] = inc(mm, "tT")

            def emit_exp_pair(p):
                j0, j1 = 2 * p, 2 * p + 1
                jb = j0 % 4
                jp = j0 % 6
                w("S", e_smm[j1], self.ld_qksc)
                if j0 < 6:
                    if h >= 1:
                        w("S", e_pv_last[h - 1])
                else:
                    w("S", e_pv[j1 - 6])
                wtot = 512 + (jobs[j1][2] - jobs[j1][1])
                e_exp[j0] = e_exp[j1] = inc(SN.activation(
                    self.pt_sb[:, jp * 512:jp * 512 + wtot],
                    self.pA[:, jb * 512:jb * 512 + wtot],
                    AF.Exp, bias=self.qksc[:, 0:1]), "tS")
                for j, jpx in ((j0, jp), (j1, jp + 1)):
                    if jobs[j][1] == 0:
                        w("V", e_exp[j], self.ld_tri)
                        e_msk[j] = inc(VN.tensor_tensor(
                            self.pt_sb[:, jpx * 512:jpx * 512 + 128],
                            self.pt_sb[:, jpx * 512:jpx * 512 + 128],
                            self.tri16[:], OP.mult), "tV")

            def emit_PV(j):
                kt, l0, l1 = jobs[j]
                jb = j % 6
                w("T", e_msk[j] if l0 == 0 else e_exp[j])
                if j == 0:
                    w("T", e_vev)
                    if h >= 1:
                        w("T", self.e_ynorm[h - 1])
                g_s = kt * 128 + l0
                g_e = kt * 128 + l1
                for ci in range(g_s // 512, (g_e + 511) // 512):
                    g0 = max(g_s, ci * 512)
                    g1 = min(g_e, (ci + 1) * 512)
                    rhs = self.pt_sb[:, jb * 512 + g0 - g_s:jb * 512 + g1 - g_s]
                    last = (j == nj - 1 and g1 == g_e)
                    if lo:
                        mm = TN.matmul(self.pB[0:65, g0:g1], self.v_lo[:, kt, :],
                                       rhs, start=(kt == 0), stop=last, skip_group_check=True)
                    else:
                        TN.matmul(self.pB[64:128, g0:g1], self.v_tmp[:, kt, :],
                                  rhs, start=(kt == 0), stop=last, skip_group_check=True,
                                  tile_position=(0, 64))
                        mm = TN.matmul(self.pB[0:64, g0:g1], self.onesb[:, 0:64],
                                       rhs, start=(kt == 0), stop=last, skip_group_check=True,
                                       tile_position=(0, 0))
                e_pv[j] = inc(mm, "tT")

            j_mid = max(jj for jj, (kt, l0, l1) in enumerate(jobs) if kt * 128 + l0 < 1024)
            p_mid = j_mid // 2 + 1
            if not EARLY_EPI:
                p_mid = 10 ** 9

            sp = 64 if lo else 0
            yr0, yr1 = (0, 64) if lo else (64, 128)
            lnrow = self.rsbc[sp:sp + 1, :]
            pbc = self.pB[yr0:yr1, 0:1024]
            e_l2 = [None, None]
            e_ln = [None, None]
            e_yev = [None, None]
            e_pb = [None, None]
            e_yn = [None, None]
            r16 = [None, None]

            def emit_epi_scalar(hf, pv_tok):
                hc = hf * 1024
                rp = sp if hf == 0 else 32
                r16[hf] = self.r16b[rp:rp + 1, :]
                psrow = (self.pB[64:65, hc:hc + 1024] if lo
                         else self.pB[0:1, hc:hc + 1024])
                w("S", pv_tok)
                if h >= 1:
                    w("S", self.e_ynorm[h - 1])
                if hf == 1:
                    w("S", e_l2[0])
                e_ln[hf] = inc(SN.activation(lnrow, psrow, AF.Ln), "tS")
                w("S", e_ln[hf], self.ld_qksc)
                if h >= 1:
                    w("S", e_pb_last[h - 1])
                e_l2[hf] = inc(SN.activation(r16[hf], lnrow, AF.Exp,
                                             scale=self.qksc[sp:sp + 1, 1:2]), "tS")
                w("V", pv_tok)
                if h >= 2:
                    w("V", self.e_ynorm[h - 2])
                e_yev[hf] = inc(VN.tensor_copy(
                    self.ytmp[yr0:yr1, hc:hc + 1024],
                    self.pB[yr0:yr1, hc:hc + 1024]), "tV")

            def emit_epi_bcast(hf):
                hc = hf * 1024
                rp = sp if hf == 0 else 32
                w("T", e_l2[hf], e_yev[hf])
                if hf == 1:
                    w("T", e_yn[0])
                for ck in range(2):
                    mm = TN.matmul(pbc[:, ck * 512:(ck + 1) * 512],
                                   self.onesb[rp:rp + 1, :],
                                   r16[hf][:, ck * 512:(ck + 1) * 512],
                                   start=True, stop=True)
                e_pb[hf] = inc(mm, "tT")
                w("V", e_pb[hf], e_yev[hf])
                ydst = self.y_fm[par * 64:(par + 1) * 64, ch, hc:hc + 1024]
                e_yn[hf] = inc(VN.tensor_tensor(
                    ydst, self.ytmp[yr0:yr1, hc:hc + 1024], pbc, OP.mult), "tV")

            for j in range(4):
                emit_S(j)
            for p in range(nj // 2):
                emit_exp_pair(p)
                if 2 * p + 4 < nj:
                    emit_S(2 * p + 4)
                if 2 * p + 5 < nj:
                    emit_S(2 * p + 5)
                emit_PV(2 * p)
                emit_PV(2 * p + 1)
                if p == p_mid:
                    emit_epi_scalar(0, e_pv[j_mid])
                if p == p_mid + 1:
                    emit_epi_bcast(0)
            e_slast[h] = e_smm[nj - 1]
            self.e_exp_last[h] = e_exp[nj - 1]
            e_pv_last[h] = e_pv[nj - 1]
            if not EARLY_EPI:
                emit_epi_scalar(0, e_pv_last[h])
                emit_epi_bcast(0)
            emit_epi_scalar(1, e_pv_last[h])
            e_yev_last[h] = e_yev[1]
            if h + 1 < H:
                e_qn = emit_qproj(h + 1, [e_exp[nj - 4 + ck] for ck in range(4)])
                e_qev[h + 1] = emit_qdup(h + 1, e_qn)
                e_kev_h[h + 1], e_kdup_h[h + 1], ekh = emit_kchain(
                    h + 1, 64 - kr, None, e_ln[1])
            emit_epi_bcast(1)
            if h + 1 < H:
                e_vev_h[h + 1] = emit_vev(
                    h + 1, emit_vproj(h + 1, e_ln[1], e_yev[1], ekh))
            e_pb_last[h] = e_pb[1]
            self.e_rsev[h] = self.e_exp_last[h]
            self.e_ynorm[h] = e_yn[1]

    def emit_wu(self):
        nc, w, inc = self.nc, self.w, self.inc
        TN, VN = nc.tensor, nc.vector
        self.ld_wu = self.gload(self.wuT_sb[:],
                                self.d_wuT.rearrange("p (c o) -> p c o", c=DC))
        self.e_wuadd = [None] * NT
        e_badd = [None] * NT
        for tt in range(NT):
            w("V", self.e_bias)
            e_badd[tt] = inc(VN.tensor_tensor(self.acc[:, tt, :], self.acc[:, tt, :],
                                              self.bias_bc[:], OP.add), "tV")
        for tt in range(NT):
            w("T", self.e_ynorm[H - 1], self.ld_wu)
            if tt <= 1:
                w("T", self.e_rsev[H - 1])
            if tt >= 2:
                w("T", self.e_wuadd[tt - 2])
            for dc in range(DC):
                for c2 in range(2):
                    o0 = (tt % 2) * 1024 + c2 * 512
                    mm = TN.matmul(self.pA[:, o0:o0 + 512],
                                   self.y_fm[:, dc, tt * 128:(tt + 1) * 128],
                                   self.wuT_sb[:, dc, c2 * 512:(c2 + 1) * 512],
                                   start=(dc == 0), stop=(dc == DC - 1),
                                   skip_group_check=True)
            e_wu = inc(mm, "tT")
            w("V", e_wu, e_badd[tt])
            self.e_wuadd[tt] = inc(VN.tensor_tensor(
                self.acc[:, tt, :], self.acc[:, tt, :],
                self.pA[:, (tt % 2) * 1024:(tt % 2 + 1) * 1024], OP.add), "tV")
        if self.debug:
            w("Y", self.e_wuadd[NT - 1], self.e_ynorm[H - 1])
            self.sload(self.d_dbg_y, self.y_fm.ap().rearrange("p c t -> p (c t)"))
            self.e_dbgacc = self.sload(self.d_dbg_acc, self.acc.ap().rearrange("p c t -> p (c t)"))

    def emit_ffn(self):
        nc, w, inc = self.nc, self.w, self.inc
        TN, VN, SN, GN = nc.tensor, nc.vector, nc.scalar, nc.gpsimd
        self.gload(self.wiq[1][:], self.d_wiT[:, 8192:16384]
                   .rearrange("p (c f) -> p c f", c=DC))
        self.ld_w[1] = self.gload(self.woq[1][:], self.d_woT[:, 8192:16384]
                                  .rearrange("p (c f) -> p c f", c=8))
        ld_bcat = self.gload(self.bcat_sb[:], self.d_bcatT)
        e_relu = {}
        e_wo = {}
        e_add = {}
        e_st = []
        e_base = {}
        seq = [(q, tq) for q in range(4) for tq in range(4)]

        def emit_base(i):
            q, tq = seq[i]
            qb = q % 2
            ib = i % 2
            for fc in range(8):
                gfc = q * 8 + fc
                pbv = self.pB[:, (fc % 2) * 512:(fc % 2 + 1) * 512]
                w("T", self.ld_w[q], self.e_zm, ld_bcat)
                if i == 0 and fc < 2:
                    w("T", self.e_ynorm[H - 1])
                else:
                    prev = ((q, tq, fc - 2) if fc >= 2 else
                            (seq[i - 1][0], seq[i - 1][1], fc + 6))
                    w("T", e_relu[prev])
                for dc in range(DC):
                    TN.matmul(pbv, self.wiq[qb][:, dc, fc * 128:(fc + 1) * 128],
                              self.xh_fm[:, dc, tq * 512:(tq + 1) * 512],
                              start=(dc == 0), stop=False, skip_group_check=True)
                mm = TN.matmul(pbv, self.bcat_sb[:, gfc * 128:(gfc + 1) * 128],
                               self.zm_sb[:, tq * 512:(tq + 1) * 512],
                               start=False, stop=True, skip_group_check=True)
                e_base[(q, tq, fc)] = inc(mm, "tT")
                w("S", e_base[(q, tq, fc)], self.ld_wib)
                if i >= 2:
                    w("S", e_wo[(seq[i - 2][0], seq[i - 2][1], 3)])
                e_relu[(q, tq, fc)] = inc(
                    SN.activation(self.inter[ib][:, fc, :], pbv, AF.Relu,
                                  bias=self.wib_sb[:, gfc:gfc + 1]), "tS")

        def emit_wo(i):
            q, tq = seq[i]
            qb = q % 2
            ib = i % 2
            for t4 in range(4):
                tt = tq * 4 + t4
                w("T", e_relu[(q, tq, 7)])
                if q == 0:
                    w("T", self.e_wuadd[tt])
                else:
                    w("T", e_add[(q - 1, 14 + (tt % 2))])
                if (q, tt - 2) in e_add:
                    w("T", e_add[(q, tt - 2)])
                for fc in range(8):
                    for c2 in range(2):
                        o0 = (tt % 2) * 1024 + c2 * 512
                        mm = TN.matmul(self.pA[:, o0:o0 + 512],
                                       self.inter[ib][:, fc, t4 * 128:(t4 + 1) * 128],
                                       self.woq[qb][:, fc, c2 * 512:(c2 + 1) * 512],
                                       start=(fc == 0), stop=(fc == 7),
                                       skip_group_check=True)
                e_wo[(q, tq, t4)] = inc(mm, "tT")
                w("V", e_wo[(q, tq, t4)])
                if self.debug:
                    w("V", self.e_dbgacc)
                e_add[(q, tt)] = inc(VN.tensor_tensor(
                    self.acc[:, tt, :], self.acc[:, tt, :],
                    self.pA[:, (tt % 2) * 1024:(tt % 2 + 1) * 1024], OP.add), "tV")
                if q == 3:
                    w("Y", e_add[(q, tt)])
                    e_st.append(self.sload(self.d_out_t[tt], self.acc[:, tt, :]))

        emit_base(0)
        for i in range(16):
            if i + 1 < 16:
                emit_base(i + 1)
            emit_wo(i)
            q, tq = seq[i]
            if tq == 3 and q + 2 <= 3:
                w("G", e_base[(q, 3, 7)], e_wo[(q, 3, 3)])
                nq_ = q + 2
                self.gload(self.wiq[q % 2][:], self.d_wiT[:, nq_ * 8192:(nq_ + 1) * 8192]
                           .rearrange("p (c f) -> p c f", c=DC))
                self.ld_w[nq_] = self.gload(
                    self.woq[q % 2][:], self.d_woT[:, nq_ * 8192:(nq_ + 1) * 8192]
                    .rearrange("p (c f) -> p c f", c=8))
        w("Y", e_st[-1])


def build_nc(debug=False):
    kb = KB(debug=debug)
    return kb.build()


def host_inputs(L0_attn_Wq, L0_attn_Wk, L0_attn_Wv, L0_attn_Wu, L0_attn_bu,
                L0_ffn_gate_W, L0_ffn_gate_b, L0_ffn_wi_W, L0_ffn_wi_b,
                L0_ffn_wo_W, L0_ffn_wo_b, L0_ffn_lora_A, L0_ffn_lora_B):
    f32 = lambda a: np.ascontiguousarray(np.asarray(a, dtype=np.float32))
    Wq, Wk, Wv, Wu = f32(L0_attn_Wq), f32(L0_attn_Wk), f32(L0_attn_Wv), f32(L0_attn_Wu)
    gw, wi, wo = f32(L0_ffn_gate_W), f32(L0_ffn_wi_W), f32(L0_ffn_wo_W)
    la, lb = f32(L0_ffn_lora_A), f32(L0_ffn_lora_B)
    d = {}
    d["wqT"] = np.ascontiguousarray(Wq.T)
    d["wkT"] = np.ascontiguousarray(Wk.T)
    d["wvT"] = np.ascontiguousarray(Wv.T)
    d["wuT"] = np.ascontiguousarray(
        Wu.T.reshape(DC, 128, D).transpose(1, 0, 2).reshape(128, DC * D))
    d["bu"] = f32(L0_attn_bu).reshape(1, D)
    d["gwT"] = np.ascontiguousarray(
        gw.T.reshape(DC, 128, E).transpose(1, 0, 2).reshape(128, DC * E))
    d["gb"] = f32(L0_ffn_gate_b).reshape(E, 1)
    d["wiT"] = np.ascontiguousarray(
        wi.reshape(4, 1024, DC, 128).transpose(3, 0, 2, 1).reshape(128, 4 * DC * 1024))
    d["wib"] = np.ascontiguousarray(f32(L0_ffn_wi_b).reshape(FF // 128, 128).T)
    d["woT"] = np.ascontiguousarray(
        wo.T.reshape(4, 8, 128, 1024).transpose(2, 0, 1, 3).reshape(128, 4 * 8 * 1024))
    d["wob"] = f32(L0_ffn_wo_b).reshape(1, D)
    d["acatT"] = np.ascontiguousarray(
        la.reshape(E * R, D).T.reshape(DC, 128, E * R).transpose(1, 0, 2)
        .reshape(128, DC * E * R))
    d["bcatT"] = np.ascontiguousarray(lb.transpose(0, 2, 1).reshape(E * R, FF))
    p = np.arange(128)[:, None]
    f = np.arange(128)[None, :]
    d["cmaskT"] = np.where(f >= p, 0.0, NEG).astype(np.float32)
    d["ciota"] = np.broadcast_to(np.repeat(np.arange(E, dtype=np.float32), R),
                                 (128, E * R)).copy()
    d["cqksc"] = np.concatenate([np.full((128, 1), MSHIFT, np.float32),
                                 np.full((128, 1), -1.0, np.float32)], axis=1)
    import ml_dtypes
    d["cones32"] = np.ones((128, 128), dtype=np.float32)
    d["cones16"] = np.ones((128, NT * (C + 1)), dtype=ml_dtypes.bfloat16)
    d["cid16"] = np.eye(128).astype(ml_dtypes.bfloat16)
    d["ctri16"] = np.where(f < p, 0.0, 1.0).astype(ml_dtypes.bfloat16)
    d["cident"] = np.eye(128, dtype=np.float32)
    return d


def kernel(L0_attn_x, L0_attn_Wq, L0_attn_Wk, L0_attn_Wv, L0_attn_Wu, L0_attn_bu,
           L0_ffn_gate_W, L0_ffn_gate_b, L0_ffn_wi_W, L0_ffn_wi_b,
           L0_ffn_wo_W, L0_ffn_wo_b, L0_ffn_lora_A, L0_ffn_lora_B):
    x = np.ascontiguousarray(np.asarray(L0_attn_x, dtype=np.float32))
    shared = host_inputs(L0_attn_Wq, L0_attn_Wk, L0_attn_Wv, L0_attn_Wu, L0_attn_bu,
                         L0_ffn_gate_W, L0_ffn_gate_b, L0_ffn_wi_W, L0_ffn_wi_b,
                         L0_ffn_wo_W, L0_ffn_wo_b, L0_ffn_lora_A, L0_ffn_lora_B)
    nc = build_nc()
    in_maps = [dict(shared, x=np.ascontiguousarray(x[i])) for i in range(NCORES)]
    res = run_bass_kernel_spmd(nc, in_maps, core_ids=list(range(NCORES)))
    return np.stack([np.asarray(res.results[i]["out"], dtype=np.float32)
                     for i in range(NCORES)], axis=0)
